# version 13
# speedup vs baseline: 1.1140x; 1.0248x over previous
import numpy as np
import ml_dtypes
from contextlib import ExitStack
import concourse.bass as bass
import concourse.mybir as mybir
from concourse.bass_utils import run_bass_kernel_spmd
from concourse.alu_op_type import AluOpType as OP

F32 = mybir.dt.float32
BF16 = mybir.dt.bfloat16
AF = mybir.ActivationFunctionType
AX = mybir.AxisListType.X
LN2 = float(np.log(2.0))

T = 4096
C = 1024
NT = 32
NBLK = 8
NC8 = 8
HD = 64
NH = 16
IF = 4096

DEBUG_SCRATCH = False
BUILD_STAGE = "full"
DEV_NBLK = None
B_TRUNC = 4
kernel_tap = False
ACT_NBLK = NBLK
ACT_NT = NT
_STAGES = ["l0a", "l0b", "l0c", "l0", "w", "l1a", "l1b", "full"]


def _sge(s):
    return _STAGES.index(BUILD_STAGE) >= _STAGES.index(s)


class Prog:
    ENGS = ("gp", "sy", "te", "ve", "se")

    def __init__(self, nc):
        self.nc = nc
        self.q = {k: [] for k in self.ENGS}
        self.S = {}
        self.cnt = {}
        self.lastw = {k: {} for k in self.ENGS}

    def sem(self, stack, name):
        self.S[name] = stack.enter_context(self.nc.semaphore(name))
        self.cnt[name] = 0

    def n(self, sem):
        return self.cnt[sem]

    def emit(self, eng, fn, inc=None, waits=()):
        if inc is not None:
            self.cnt[inc[0]] += inc[1]
        lw = self.lastw[eng]
        eff = []
        for s, v in waits:
            if v > 0 and lw.get(s, -1) < v:
                eff.append((s, v))
                lw[s] = v
        S = self.S

        def run(e, fn=fn, inc=inc, eff=tuple(eff)):
            for s, v in eff:
                e.wait_ge(S[s], v)
            inst = fn(e)
            if inc is not None:
                inst.then_inc(S[inc[0]], inc[1])
        self.q[eng].append(run)


def build_program():
    global ACT_NBLK, ACT_NT
    kernel_tap = DEBUG_SCRATCH
    ACT_NBLK = DEV_NBLK or NBLK
    ACT_NT = 4 * ACT_NBLK
    nc = bass.Bass("TRN2", target_bir_lowering=False)
    kind_s = "ExternalOutput" if DEBUG_SCRATCH else "Internal"

    di = lambda n, sh: nc.dram_tensor(n, sh, F32, kind="ExternalInput").ap()
    x_in = di("x", [T, C])
    l0_wqkvT = di("l0_wqkvT", [C, 3 * C])
    l0_wprojT = di("l0_wprojT", [C, C])
    l0_bproj = di("l0_bproj", [1, C])
    rt_w = di("rt_w", [C, 8])
    rt_b = di("rt_b", [1, 8])
    moe_w1 = di("moe_w1", [10, C, 512])
    moe_w3 = di("moe_w3", [10, C, 512])
    moe_w2 = di("moe_w2", [10, 512, C])
    l1_wq = di("l1_wq", [C, C])
    l1_wk = di("l1_wk", [C, C])
    l1_wv = di("l1_wv", [C, C])
    l1_wo = di("l1_wo", [C, C])
    l1_bo = di("l1_bo", [1, C])
    l1_wf2 = di("l1_wf2", [128, HD])
    l1_bf2 = di("l1_bf2", [128, 1])
    gate_w = di("gate_w", [IF, C])
    up_w = di("up_w", [IF, C])
    down_w = di("down_w", [C, IF])
    s_in_rep = di("s_in_rep", [128, 1])
    s_ind_rep = di("s_ind_rep", [128, 1])
    ident_in = di("ident", [128, 128])

    out = nc.dram_tensor("out", [T, C], F32, kind="ExternalOutput").ap()

    ds = lambda n, sh, dt=BF16: nc.dram_tensor(n, sh, dt, kind=kind_s).ap()
    sc_xnT = ds("sc_xnT", [NC8, 128, T])
    sc_qT = ds("sc_qT", [NC8, 128, T])
    sc_s = ds("sc_s", [NT, 128, C])
    sc_x1 = ds("sc_x1", [NT, 128, C])
    sc_xqT = ds("sc_xqT", [NC8, 128, T])
    sc_hqT = ds("sc_hqT", [NT, 32, 128, 128])
    sc_gq = ds("sc_gq", [NC8, 128, IF])
    sc_uq = ds("sc_uq", [NC8, 128, IF])
    sc_dq = ds("sc_dq", [32, 128, C])
    sc_p128 = nc.dram_tensor("sc_p128", [1, 128], F32, kind=kind_s).ap()
    sc_tap = nc.dram_tensor("sc_tap", [6, 128, 512], F32, kind=kind_s).ap()

    with ExitStack() as st:
        sb = lambda n, sh, dt=F32: st.enter_context(nc.sbuf_tensor(n, sh, dt))
        identb = sb("identb", [128, 128], BF16)
        ones_row = sb("ones_row", [1, 128], BF16)
        ones_f32 = sb("ones_f32", [1, 128])
        eps_ln = sb("eps_ln", [128, 1])
        junk = sb("junk", [128, 64])
        sin_r = sb("sin_r", [128, 1])
        sind_r = sb("sind_r", [128, 1])
        sin6 = sb("sin6", [128, 1])
        sind6 = sb("sind6", [128, 1])
        wq_gs = {w: sb(f"wq_gs_{w}", [128, 1]) for w in ("g", "u", "d")}
        wq_gs6 = {w: sb(f"wq_gs6_{w}", [128, 1]) for w in ("g", "u", "d")}
        r_g = sb("r_g", [128, 1])
        r_u = sb("r_u", [128, 1])
        r_d = sb("r_d", [128, 1])
        bias_row = sb("bias_row", [1, C], BF16)
        rtb_row = sb("rtb_row", [1, 8], BF16)
        wf_sb = sb("wf_sb", [128, HD], BF16)
        bf_sb = sb("bf_sb", [128, 1])
        arena_t = sb("arena", [128, 57344], BF16)
        arena = arena_t[:]
        wres = arena[:, 0:24576]
        wres2 = arena[:, 24576:32768]
        wbufA = arena[:, 32768:36864]
        wbufB = arena[:, 36864:40960]
        wbufC = arena[:, 40960:45056]
        wbufD = arena[:, 45056:49152]
        qT_blks = [arena[:, 49152:53248], arena[:, 53248:57344]]
        downq = arena[:, 0:32768]
        rtw_sb = sb("rtw_sb", [128, 64], BF16)
        xnT_blks = [sb("xnT_blk0", [128, 4096], BF16),
                    sb("xnT_blk1", [128, 4096], BF16)]
        accbuf = arena[:, 0:16384]
        x_sb = [sb(f"x_sb{i}", [128, C]) for i in range(2)]
        xn_sb = sb("xn_sb", [128, C])
        xnb_sb = sb("xnb_sb", [128, C], BF16)
        stat = {n: sb(f"st_{n}", [128, 1]) for n in
                ("s1", "s2", "m", "var", "lnv", "rstd")}

        vb_sb = sb("vb_sb", [128, NH * (HD + 1)], BF16)
        kv_sb = sb("kv_sb", [128, NH * (HD + 1)], BF16)
        attn_sb = sb("attn_sb", [128, C], BF16)
        s_sb = sb("s_sb", [128, C], BF16)
        rn_sb = sb("rn_sb", [128, NH])
        rn2_sb = sb("rn2_sb", [128, NH])
        gates_sb = sb("gates_sb", [128, 8 * NT])
        gwl = sb("gwl", [128, 8])
        gwt = sb("gwt", [128, 8])
        gww = sb("gww", [128, 8])
        gs1 = sb("gs1", [128, 1])
        gs2 = sb("gs2", [128, 1])
        h_sb = sb("h_sb", [128, 512], BF16)
        hT_sb = sb("hT_sb", [128, 512], BF16)
        g_sb = sb("g_sb", [128, C])
        sil_sb = sb("sil_sb", [128, 512])
        x1_sb = sb("x1_sb", [128, C], BF16)
        x2_sb = sb("x2_sb", [128, C])
        kf_sb = sb("kf_sb", [128, 4096], BF16)
        kphi_sb = kf_sb[:, 0:1024]
        attnT_sb = kf_sb[:, 1024:2048]
        ktmp_sb = sb("ktmp_sb", [128, 512], BF16)
        qtb_sb = sb("qtb_sb", [128, 512], BF16)
        tmpA = sb("tmpA", [128, 1024])
        tmpB = tmpA
        sq_sb = tmpA
        quZ = {n: sb(f"qu_{n}", [128, 1024]) for n in ("SU", "SQ", "NA", "NB")}
        quZ["G"] = quZ["NB"]
        quB = {n: sb(f"qb_{n}", [128, 64]) for n in
               ("A", "SC", "L2", "E", "PP", "R", "RQ", "S8", "RS", "T0")}
        qout_sb = None
        wq_in = [xn_sb, x_sb[0]]
        wmax_sb = sb("wmax_sb", [128, 1])
        tmax_sb = sb("tmax_sb", [128, 1])
        m128_sb = sb("m128_sb", [1, 128])
        gmax_sb = sb("gmax_sb", [128, 1])
        hq_stage = sb("hq_stage", [128, 2048], BF16)
        qout_sb = g_sb[:].bitcast(BF16)[:, 0:1024]
        xb_sb = [x_sb[i][:].bitcast(BF16)[:, 0:C] for i in range(2)]
        psA = st.enter_context(nc.psum_tensor("psA", [128, 512], F32))
        psB = st.enter_context(nc.psum_tensor("psB", [128, 512], F32))
        psC = st.enter_context(nc.psum_tensor("psC", [128, 512], F32))
        psD = st.enter_context(nc.psum_tensor("psD", [128, 512], F32))
        psE = st.enter_context(nc.psum_tensor("psE", [128, 512], F32))
        psF = st.enter_context(nc.psum_tensor("psF", [128, 512], F32))
        psT = st.enter_context(nc.psum_tensor("psT", [128, 128], BF16))
        psS = st.enter_context(nc.psum_tensor("psS", [128, 8], F32))

        P = Prog(nc)
        for s in ("GX0", "GX1", "GWA", "GWB", "SBK0", "SBK1", "SQ0", "SQ1",
                  "SMS", "SS", "SSG", "SWG", "SXN0", "SXN1", "SQT0", "SQT1",
                  "SST", "TE", "VE", "SE"):
            P.sem(st, s)

        def gp_dma(dst, src, sem="GWA", waits=()):
            P.emit("gp", lambda e, d=dst, s=src: e.dma_start(out=d, in_=s),
                   inc=(sem, 16), waits=waits)

        def sy_load(dst, src, sem="SMS", waits=()):
            P.emit("sy", lambda e, d=dst, s=src: e.dma_start(out=d, in_=s),
                   inc=(sem, 16), waits=waits)

        def sy_store(dst, src, sem="SS", waits=()):
            P.emit("sy", lambda e, d=dst, s=src: e.dma_start(out=d, in_=s),
                   inc=(sem, 16), waits=waits)

        def sy_barrier(*sems):
            for sm in sems:
                P.emit("sy", lambda e, sm=sm, v=P.n(sm): e.wait_ge(P.S[sm], v)
                       if v > 0 else e.nop(), inc=None)

        def ve(fn, inc=True, waits=()):
            P.emit("ve", fn, inc=("VE", 1) if inc else None, waits=waits)

        def se(fn, inc=True, waits=()):
            P.emit("se", fn, inc=("SE", 1) if inc else None, waits=waits)

        def te(fn, inc=True, waits=()):
            P.emit("te", fn, inc=("TE", 1) if inc else None, waits=waits)

        def ve_spacer():
            ve(lambda e: e.tensor_scalar(junk[:], junk[:], 0.0, None, OP.mult),
               inc=False)

        def ve_spacer2():
            ve_spacer()
            ve_spacer()

        def ve_fence():
            ve_spacer()
            ve(lambda e: e.tensor_scalar(junk[:], junk[:], 0.0, None, OP.mult))

        def se_fence():
            se(lambda e: e.activation(junk[:, 0:8], junk[:, 0:8], AF.Copy))

        def vcopy(dst, src, inc=True, waits=()):
            ve(lambda e, d=dst, s=src: e.tensor_copy(d, s), inc=inc, waits=waits)

        def vtt(dst, a, b, op, inc=True, waits=()):
            ve(lambda e, d=dst, a=a, b=b, op=op: e.tensor_tensor(d, a, b, op),
               inc=inc, waits=waits)

        def vts(dst, a, s1, s2, op0, op1=None, inc=True, waits=()):
            if op1 is None:
                ve(lambda e, d=dst, a=a, s1=s1, op0=op0:
                   e.tensor_scalar(d, a, s1, None, op0), inc=inc, waits=waits)
            else:
                ve(lambda e, d=dst, a=a, s1=s1, s2=s2, op0=op0, op1=op1:
                   e.tensor_scalar(d, a, s1, s2, op0, op1), inc=inc, waits=waits)

        def vstt(dst, a, s, b, op0, op1, inc=True, waits=()):
            ve(lambda e, d=dst, a=a, s=s, b=b, op0=op0, op1=op1:
               e.scalar_tensor_tensor(d, a, s, b, op0, op1), inc=inc, waits=waits)

        def vred(dst, src, op, absv=False, inc=True, waits=()):
            ve(lambda e, d=dst, s=src, op=op, av=absv:
               e.tensor_reduce(d, s, AX, op, apply_absolute_value=av),
               inc=inc, waits=waits)

        def vrec(dst, src, inc=False, waits=()):
            ve(lambda e, d=dst, s=src: e.reciprocal(d, s), inc=inc, waits=waits)
            ve_spacer()
            ve_spacer()

        def sact(dst, src, func, bias=None, scale=1.0, inc=True, waits=()):
            if bias is None:
                se(lambda e, d=dst, s=src, f=func, sc=scale:
                   e.activation(d, s, f, scale=sc), inc=inc, waits=waits)
            else:
                se(lambda e, d=dst, s=src, f=func, b=bias, sc=scale:
                   e.activation(d, s, f, bias=b, scale=sc), inc=inc, waits=waits)

        def tmm(ps_ap, lhsT, rhs, start, stop, inc=False, waits=()):
            te(lambda e, o=ps_ap, l=lhsT, r=rhs, s1=start, s2=stop:
               e.matmul(o, l, r, start=s1, stop=s2), inc=inc, waits=waits)

        def ttr(ps_ap, src, inc=True, waits=()):
            te(lambda e, o=ps_ap, s=src: e.transpose(o, s, identb[:]),
               inc=inc, waits=waits)

        W = lambda *pairs: list(pairs)
        veW = lambda: W(("VE", P.n("VE")))
        teW = lambda: W(("TE", P.n("TE")))
        seW = lambda: W(("SE", P.n("SE")))

        gp_dma(identb[:], ident_in)
        gp_dma(sin_r[:], s_in_rep)
        gp_dma(sind_r[:], s_ind_rep)
        gp_dma(wf_sb[:], l1_wf2)
        gp_dma(bf_sb[:], l1_bf2)
        ve(lambda e: e.memset(ones_row[:], 1.0), inc=False)
        ve(lambda e: e.memset(ones_f32[:], 1.0), inc=False)
        ve(lambda e: e.memset(eps_ln[:], 1e-5), inc=False)
        ve(lambda e: e.memset(junk[:], 0.0), inc=False)
        ve(lambda e: e.memset(vb_sb[:], 1.0), inc=False)

        def emit_quant(src_ap, ncols, dst_bf, gs_ap, gs6_ap, waits_first=()):
            nb = ncols // 16
            A, SC, L2, E, PP, R, RQ, S8, RS, T0 = (
                quB["A"][:, :nb], quB["SC"][:, :nb], quB["L2"][:, :nb],
                quB["E"][:, :nb], quB["PP"][:, :nb], quB["R"][:, :nb],
                quB["RQ"][:, :nb], quB["S8"][:, :nb], quB["RS"][:, :nb],
                quB["T0"][:, :nb])
            SU, SQ, NA, NB_, G = (quZ["SU"][:, :ncols], quZ["SQ"][:, :ncols],
                                  quZ["NA"][:, :ncols], quZ["NB"][:, :ncols],
                                  quZ["G"][:, :ncols])
            SGN = x2_sb[:, :ncols]
            x3 = src_ap.rearrange("p (b k) -> p b k", k=16)
            lad = quZ["NA"]
            vred(A, x3, OP.max, absv=True, inc=False, waits=waits_first)
            ve_spacer2()
            vts(SC, A, gs6_ap, 2.0 ** -20, OP.mult, OP.max, inc=False)
            ve_spacer2()
            vts(SC, SC, 448.0, None, OP.min, inc=False)
            ve_fence()
            sact(L2, SC, AF.Ln, inc=False, waits=veW())
            se_fence()
            for ki, k in enumerate(range(-5, 9)):
                vts(lad[:, :nb * 14].rearrange("p (b j) -> p b j", j=14)
                    [:, :, ki],
                    L2, float(k) * LN2, -9.0 if ki == 0 else None,
                    OP.is_ge, OP.add if ki == 0 else None,
                    inc=False, waits=seW() if ki == 0 else ())
            ve_spacer2()
            vred(E, lad[:, :nb * 14].rearrange("p (b j) -> p b j", j=14),
                 OP.add, inc=False)
            ve_fence()
            sact(PP, E, AF.Exp, scale=-LN2, inc=False, waits=veW())
            se_fence()
            vtt(R, SC, PP, OP.mult, inc=False, waits=seW())
            ve_spacer2()
            for ki in range(16):
                vts(lad[:, :nb * 16].rearrange("p (b j) -> p b j", j=16)
                    [:, :, ki],
                    R, ki + 0.5, None, OP.is_ge, inc=False)
            ve_spacer2()
            vred(RQ, lad[:, :nb * 16].rearrange("p (b j) -> p b j", j=16),
                 OP.add, inc=False)
            ve_spacer2()
            vrec(T0, R)
            vtt(T0, T0, RQ, OP.mult, inc=False)
            ve_spacer2()
            vtt(S8, T0, SC, OP.mult, inc=False)
            ve_spacer2()
            vts(T0, S8, 2.0 ** -30, None, OP.max, inc=False)
            ve_spacer2()
            vrec(RS, T0)
            vts(RS, RS, gs_ap, None, OP.mult, inc=False)
            ve_spacer2()
            rs_b = RS.unsqueeze(2).broadcast_to([128, nb, 16])
            s8_b = S8.unsqueeze(2).broadcast_to([128, nb, 16])
            su3 = SU.rearrange("p (b k) -> p b k", k=16)
            vtt(su3, x3, rs_b, OP.mult, inc=False)
            ve_fence()
            sact(SGN, SU, AF.Sign, inc=False, waits=veW())
            se_fence()
            vtt(SQ, SU, SU, OP.mult, inc=False)
            vts(NA, SQ, 0.0625, None, OP.is_ge, inc=False)
            for c in (0.75, 1.25, 1.75):
                vstt(NA, SQ, c * c, NA, OP.is_ge, OP.add, inc=False)
            vts(NB_, SQ, 6.25, None, OP.is_ge, inc=False)
            vstt(NB_, SQ, 12.25, NB_, OP.is_ge, OP.add, inc=False)
            vstt(G, NA, 0.5, NB_, OP.mult, OP.add, inc=False)
            vts(NA, SQ, 25.0, None, OP.is_ge, inc=False)
            vstt(G, NA, 2.0, G, OP.mult, OP.add, inc=False)
            vtt(G, SGN, G, OP.mult, inc=False, waits=seW())
            g3 = G.rearrange("p (b k) -> p b k", k=16)
            o3 = dst_bf.rearrange("p (b k) -> p b k", k=16)
            vtt(o3, g3, s8_b, OP.mult)
            return ("VE", P.n("VE"))

        def emit_ln(x_ap, out_f32, out_bf, xwait):
            sact(sq_sb[:], x_ap, AF.Square, waits=xwait)
            vred(stat["s1"][:], x_ap, OP.add, inc=False, waits=xwait)
            vred(stat["s2"][:], sq_sb[:], OP.add, inc=False, waits=seW())
            vts(stat["m"][:], stat["s1"][:], 1.0 / C, None, OP.mult, inc=False)
            ve_spacer()
            vtt(stat["var"][:], stat["m"][:], stat["m"][:], OP.mult, inc=False)
            ve_spacer()
            vstt(stat["var"][:], stat["s2"][:], 1.0 / C, stat["var"][:],
                 OP.mult, OP.subtract, inc=False)
            ve_fence()
            sact(stat["lnv"][:], stat["var"][:], AF.Ln, bias=eps_ln[:],
                 inc=False, waits=veW())
            sact(stat["rstd"][:], stat["lnv"][:], AF.Exp, scale=-0.5,
                 inc=False)
            se_fence()
            if out_f32 is not None:
                vts(out_f32, x_ap, stat["m"][:], stat["rstd"][:],
                    OP.subtract, OP.mult, inc=False, waits=seW())
                vcopy(out_bf, out_f32)
            else:
                vts(out_bf, x_ap, stat["m"][:], stat["rstd"][:],
                    OP.subtract, OP.mult, waits=seW())
            return ("VE", P.n("VE"))

        def emit_phi(ps_ap, dst_bf, t1, t2, mmwait):
            sact(t1, ps_ap, AF.Relu, scale=-1.0, waits=mmwait + veW())
            sact(t2, t1, AF.Exp, scale=-1.0)
            vts(t1, ps_ap, 0.0, None, OP.max, inc=False, waits=seW())
            vtt(dst_bf, t2, t1, OP.add)

        def phase_A(layer):
            x_slot_done = [0, 0]
            psAb = psA[:].bitcast(BF16)[:, 0:128]
            psTse = [0, 0]
            for b in range(ACT_NBLK):
                xblk = xnT_blks[b % 2]
                sxn = "SXN%d" % (b % 2)
                sqt = "SQT%d" % (b % 2)
                xblk_guard = W((sxn, P.n(sxn)))
                qblk_guard = W((sqt, P.n(sqt)))
                for ti in range(4):
                    tt = b * 4 + ti
                    gw = W(("VE", x_slot_done[tt % 2])) if tt >= 2 else ()
                    if layer == 0:
                        xs = x_sb[tt % 2]
                        sx = "GX%d" % (tt % 2)
                        gp_dma(xs[:], x_in[tt * 128:(tt + 1) * 128, :],
                               sem=sx, waits=gw)
                        xw = W((sx, P.n(sx)))
                        emit_ln(xs[:], None, xnb_sb[:], xw)
                    else:
                        xs = xb_sb[tt % 2]
                        sx = "SBK%d" % (tt % 2)
                        sy_load(xs, sc_x1[tt, :, :], sem=sx, waits=gw)
                        xw = W((sx, P.n(sx)))
                        emit_ln(xs, xn_sb[:], xnb_sb[:], xw)
                    x_slot_done[tt % 2] = P.n("VE")
                    lnd = ("VE", P.n("VE"))
                    for cc in range(NC8):
                        bk = psT[:] if cc % 2 == 0 else psAb
                        ttr(bk, xnb_sb[:, cc * 128:(cc + 1) * 128],
                            waits=W(lnd, ("VE", psTse[cc % 2])))
                        vcopy(xblk[:, cc * 512 + ti * 128:
                                   cc * 512 + (ti + 1) * 128],
                              bk, waits=teW() + xblk_guard)
                        psTse[cc % 2] = P.n("VE")
                    if layer == 1:
                        qd = emit_quant(xn_sb[:], C, qout_sb,
                                        sin_r[:, 0:1], sin6[:, 0:1],
                                        waits_first=W(lnd))
                        sguard = W(("SSG", P.n("SSG"))) if tt >= 1 else []
                        for cc in range(NC8):
                            bk = psT[:] if cc % 2 == 0 else psAb
                            ttr(bk, qout_sb[:, cc * 128:(cc + 1) * 128],
                                waits=W(qd, ("VE", psTse[cc % 2])))
                            vcopy(hq_stage[:, cc * 128:(cc + 1) * 128], bk,
                                  waits=teW() + sguard)
                            psTse[cc % 2] = P.n("VE")
                        sy_store(sc_xqT[:, :, tt * 128:(tt + 1) * 128]
                                 .rearrange("c p n -> p c n"),
                                 hq_stage[:, 0:C].rearrange(
                                     "p (c n) -> p c n", n=128),
                                 sem="SSG", waits=veW())
                xblk_done = ("VE", P.n("VE"))
                for cc in range(NC8):
                    sy_store(sc_xnT[cc, :, b * 512:(b + 1) * 512],
                             xblk[:, cc * 512:(cc + 1) * 512], sem=sxn,
                             waits=W(xblk_done) if cc == 0 else ())
                qblk = qT_blks[b % 2]
                for eq in range(NC8):
                    ps = psA if eq % 2 == 0 else psB
                    for cc in range(NC8):
                        if layer == 0:
                            lhs = wres[:, cc * 3072 + eq * 128:
                                       cc * 3072 + eq * 128 + 128]
                        else:
                            lhs = wres[:, cc * 3072 + eq * 128:
                                       cc * 3072 + eq * 128 + 128]
                        tmm(ps[:], lhs, xblk[:, cc * 512:(cc + 1) * 512],
                            cc == 0, cc == 7, inc=(cc == 7),
                            waits=(W(xblk_done) + veW()) if cc == 0 else ())
                    mmd = teW() + (qblk_guard if eq == 0 else [])
                    qsl = qblk[:, eq * 512:(eq + 1) * 512]
                    if layer == 0:
                        emit_phi(ps[:], qsl, tmpA[:, 0:512],
                                 tmpA[:, 512:1024], mmd)
                    else:
                        vcopy(ktmp_sb[:], ps[:], waits=mmd)
                        qevd = veW()
                        for half in (0, 1):
                            tmm(psB[half * 64:half * 64 + 64, :],
                                wf_sb[half * 64:half * 64 + 64, :],
                                ktmp_sb[half * 64:half * 64 + 64, :],
                                True, True, inc=(half == 1),
                                waits=(qevd + veW() + seW()) if half == 0 else ())
                        sact(qsl, psB[:], AF.Relu, bias=bf_sb[:],
                             waits=teW() + (qblk_guard if eq == 0 else []))
                        ve(lambda e: e.tensor_copy(junk[:], junk[:]),
                           waits=seW())
                qT_done = ("VE", P.n("VE")) if layer == 0 else \
                    ("SE", P.n("SE"))
                for eq in range(NC8):
                    sy_store(sc_qT[eq, :, b * 512:(b + 1) * 512],
                             qblk[:, eq * 512:(eq + 1) * 512], sem=sqt,
                             waits=W(qT_done) if eq == 0 else ())
                if layer == 1:
                    for ek in range(NC8):
                        for cc in range(NC8):
                            lhs = wres[:, cc * 3072 + 1024 + ek * 128:
                                       cc * 3072 + 1024 + ek * 128 + 128]
                            tmm(psA[:], lhs, xblk[:, cc * 512:(cc + 1) * 512],
                                cc == 0, cc == 7, inc=(cc == 7),
                                waits=(veW() + seW()) if cc == 0 else ())
                        vcopy(ktmp_sb[:], psA[:], waits=teW())
                        kevd = veW()
                        for h2 in (0, 1):
                            tmm(psB[h2 * 64:h2 * 64 + 64, :],
                                wf_sb[h2 * 64:h2 * 64 + 64, :],
                                ktmp_sb[h2 * 64:h2 * 64 + 64, :],
                                True, True, inc=(h2 == 1),
                                waits=(kevd + veW() + seW())
                                if h2 == 0 else ())
                        sact(qtb_sb[:], psB[:], AF.Relu, bias=bf_sb[:],
                             waits=teW())
                        se_mark = seW()
                        for ti4 in range(4):
                            bk = psT[:] if ti4 % 2 == 0 else psAb
                            ttr(bk, qtb_sb[:, ti4 * 128:(ti4 + 1) * 128],
                                waits=se_mark + W(("VE", psTse[ti4 % 2])))
                            vcopy(kf_sb[:, ti4 * 1024 + ek * 128:
                                        ti4 * 1024 + (ek + 1) * 128],
                                  bk, waits=teW())
                            psTse[ti4 % 2] = P.n("VE")
                for ti in range(4):
                    tt = b * 4 + ti
                    xts = [xblk[:, cc * 512 + ti * 128:
                                cc * 512 + (ti + 1) * 128] for cc in range(NC8)]
                    if layer == 0:
                        for half, ps in ((0, psA), (1, psB)):
                            for cc in range(NC8):
                                rhs = wres[:, cc * 3072 + 1024 + half * 512:
                                           cc * 3072 + 1024 + (half + 1) * 512]
                                tmm(ps[:], xts[cc], rhs, cc == 0, cc == 7,
                                    inc=(cc == 7),
                                    waits=veW() if cc == 0 else ())
                        kmm = teW()
                        emit_phi(psA[:], kphi_sb[:, 0:512], tmpA[:, 0:512],
                                 tmpA[:, 512:1024], kmm)
                        emit_phi(psB[:], kphi_sb[:, 512:1024], tmpA[:, 0:512],
                                 tmpA[:, 512:1024], kmm)
                        kdone = ("VE", P.n("VE"))
                    else:
                        kdone = ("VE", P.n("VE"))
                    for half, ps in ((0, psA), (1, psB)):
                        for cc in range(NC8):
                            rhs = wres[:, cc * 3072 + 2048 + half * 512:
                                       cc * 3072 + 2048 + (half + 1) * 512]
                            tmm(ps[:], xts[cc], rhs, cc == 0, cc == 7,
                                inc=(cc == 7),
                                waits=(W(kdone) + veW()) if cc == 0 else ())
                    vmm = teW()
                    vb3 = vb_sb[:].rearrange("p (j k) -> p j k", k=HD + 1)
                    for half, ps in ((0, psA), (1, psB)):
                        ve(lambda e, d=vb3[:, 8 * half:8 * (half + 1), 0:HD],
                           s=ps[:].rearrange("p (j k) -> p j k", k=HD):
                           e.tensor_copy(d, s),
                           inc=(half == 1), waits=vmm)
                    vb_done = ("VE", P.n("VE"))
                    if layer == 0:
                        for cc in range(NC8):
                            tmm(psS[:], xts[cc], rtw_sb[:, cc * 8:(cc + 1) * 8],
                                cc == 0, False,
                                waits=veW() if cc == 0 else ())
                        tmm(psS[:], ones_row[:], rtb_row[:], False, True,
                            inc=True)
                        rtmm = teW()
                    kv_banks = (psC, psD, psE, psF)
                    for h in range(NH):
                        ps = kv_banks[h % 4]
                        clo = 65 * (h // 4)
                        ksl = (kphi_sb[:, h * HD:(h + 1) * HD] if layer == 0
                               else kf_sb[:, ti * 1024 + h * HD:
                                          ti * 1024 + (h + 1) * HD])
                        tmm(ps[0:64, clo:clo + 65],
                            ksl,
                            vb3[:, h, :],
                            (tt == 0 and h < 4),
                            (tt == ACT_NT - 1 and h >= 12),
                            inc=(h == NH - 1),
                            waits=W(vb_done, kdone) if h == 0 else ())
                    if layer == 0:
                        vcopy(gwl[:], psS[:], inc=False, waits=rtmm)
                        ve_spacer()
                        vred(gs1[:], gwl[:], OP.max, inc=False)
                        ve_spacer()
                        vts(gwt[:], gwl[:], gs1[:, 0:1], -1e30, OP.is_ge,
                            OP.mult, inc=False)
                        ve_spacer()
                        vtt(gwt[:], gwl[:], gwt[:], OP.add, inc=False)
                        ve_spacer()
                        vred(gs2[:], gwt[:], OP.max, inc=False)
                        ve_spacer()
                        vts(gs1[:], gs1[:], -1.0, None, OP.mult, inc=False)
                        ve_fence()
                        sact(gww[:], gwl[:], AF.Exp, bias=gs1[:, 0:1],
                             waits=veW())
                        vts(gwt[:], gwl[:], gs2[:, 0:1], None, OP.is_ge,
                            inc=False, waits=seW())
                        ve_spacer()
                        vtt(gww[:], gww[:], gwt[:], OP.mult, inc=False)
                        ve_spacer()
                        vred(gs1[:], gww[:], OP.add, inc=False)
                        ve_spacer()
                        vrec(gs2[:], gs1[:])
                        vts(gates_sb[:, tt * 8:(tt + 1) * 8], gww[:],
                            gs2[:, 0:1], None, OP.mult)
            kvw = teW()
            for bi, ps in enumerate((psC, psD, psE, psF)):
                dst = bass.AP(kv_sb, bi * 65,
                              [[NH * (HD + 1), 64], [260, 4], [1, 65]])
                src = ps[0:64, 0:260].rearrange("p (g k) -> p g k", k=65)
                vcopy(dst, src, inc=(bi == 3), waits=kvw if bi == 0 else ())


        def phase_B(layer):
            kvd = ("VE", P.n("VE"))
            sy_barrier("SS", "SXN0", "SXN1", "SQT0", "SQT1")
            qwide = arena[:, 49152:57344]
            psAb = psA[:].bitcast(BF16)[:, 0:128]
            psTse = [0, 0]
            for b in range(ACT_NBLK):
                qg = teW()
                for cc in range(NC8):
                    sy_load(qwide[0:64, (2 * cc) * 512:(2 * cc + 1) * 512],
                            sc_qT[cc, 0:64, b * 512:(b + 1) * 512],
                            sem="SQ0", waits=qg if cc == 0 else ())
                    sy_load(qwide[0:64, (2 * cc + 1) * 512:(2 * cc + 2) * 512],
                            sc_qT[cc, 64:128, b * 512:(b + 1) * 512],
                            sem="SQ0")
                qldw = W(("SQ0", P.n("SQ0")), kvd)
                for ti in range(4):
                    tt = b * 4 + ti
                    att_ps = (psC, psD, psE, psF)
                    AW = 72
                    for h in range(NH):
                        ps = att_ps[h // 4]
                        clo = AW * (h % 4)
                        tmm(ps[:, clo:clo + 65],
                            qwide[0:64, h * 512 + ti * 128:
                                  h * 512 + (ti + 1) * 128],
                            kv_sb[0:64, h * 65:h * 65 + 65],
                            True, True, inc=(h == NH - 1),
                            waits=(qldw + veW()) if h == 0 else ())
                    attmm = teW()
                    if B_TRUNC < 2:
                        vcopy(s_sb[:, 0:512], att_ps[0][:, 0:512], inc=False,
                              waits=attmm + W(("SST", P.n("SST"))))
                        vcopy(s_sb[:, 512:1024], att_ps[1][:, 0:512])
                        sy_store(sc_s[tt, :, :], s_sb[:], sem="SST",
                                 waits=veW())
                        continue
                    for q4 in range(4):
                        nv = att_ps[q4][:, 0:4 * AW].rearrange(
                            "p (j k) -> p j k", k=AW)[:, :, 64]
                        vcopy(rn_sb[:, q4 * 4:(q4 + 1) * 4], nv, inc=False,
                              waits=attmm if q4 == 0 else ())
                    ve_spacer()
                    if layer == 0:
                        vts(rn_sb[:], rn_sb[:], 1e-6, None, OP.add, inc=False)
                    else:
                        vts(rn_sb[:], rn_sb[:], 1e-6, None, OP.max, inc=False)
                    ve_spacer()
                    vrec(rn2_sb[:], rn_sb[:])
                    for h in range(NH):
                        ps = att_ps[h // 4]
                        clo = AW * (h % 4)
                        vts(attn_sb[:, h * HD:(h + 1) * HD],
                            ps[:, clo:clo + HD], rn2_sb[:, h:h + 1], None,
                            OP.mult, inc=(h == NH - 1))
                    attn_done = ("VE", P.n("VE"))
                    if B_TRUNC < 3:
                        vcopy(s_sb[:], attn_sb[:], inc=False,
                              waits=W(attn_done, ("SST", P.n("SST"))))
                        ve(lambda e: e.tensor_copy(junk[:], junk[:]))
                        sy_store(sc_s[tt, :, :], s_sb[:], sem="SST",
                                 waits=veW())
                        continue
                    for cc in range(NC8):
                        bk = psT[:] if cc % 2 == 0 else psAb
                        ttr(bk, attn_sb[:, cc * 128:(cc + 1) * 128],
                            waits=W(attn_done, ("VE", psTse[cc % 2])))
                        vcopy(attnT_sb[:, cc * 128:(cc + 1) * 128], bk,
                              waits=teW())
                        psTse[cc % 2] = P.n("VE")
                    attnT_done = ("VE", P.n("VE"))
                    if B_TRUNC < 4:
                        vcopy(s_sb[:], attnT_sb[:], inc=False,
                              waits=W(attnT_done, ("SST", P.n("SST"))))
                        ve(lambda e: e.tensor_copy(junk[:], junk[:]))
                        sy_store(sc_s[tt, :, :], s_sb[:], sem="SST",
                                 waits=veW())
                        continue
                    for half, ps in ((0, psA), (1, psB)):
                        for cc in range(NC8):
                            wc = wres2[:, cc * 1024 + half * 512:
                                       cc * 1024 + (half + 1) * 512]
                            tmm(ps[:], attnT_sb[:, cc * 128:(cc + 1) * 128],
                                wc, cc == 0, False,
                                waits=(W(attnT_done) + veW())
                                if cc == 0 else ())
                        tmm(ps[:], ones_row[:],
                            bias_row[:, half * 512:(half + 1) * 512],
                            False, True, inc=True)
                    pmm = teW()
                    vcopy(s_sb[:, 0:512], psA[:], inc=False,
                          waits=pmm + W(("SST", P.n("SST"))))
                    vcopy(s_sb[:, 512:1024], psB[:])
                    psTse[1] = P.n("VE")
                    sy_store(sc_s[tt, :, :], s_sb[:], sem="SST", waits=veW())

        def phase_C_moe(half):
            accv = accbuf.rearrange("p (t n) -> p t n", n=C)
            sy_barrier("SS", "SXN0", "SXN1", "SST")
            te_marks = []
            silm = [0, 0]; hsttm = [0, 0]; psTm = [0]; hTm = [0, 0]
            accm = [0]; trTE = [0, 0]
            xblk_teM = [0, 0]
            for ex in range(10):
                w1b = wbufA if ex % 2 == 0 else wbufB
                w3b = wbufC if ex % 2 == 0 else wbufD
                w2b = qT_blks[ex % 2]
                gwx = "GWA" if ex % 2 == 0 else "GWB"
                bw = W(("TE", te_marks[ex - 2])) if ex >= 2 else teW()
                gp_dma(w1b.rearrange("p (c n) -> p c n", n=512),
                       moe_w1[ex].rearrange("(c p) n -> p c n", p=128),
                       sem=gwx, waits=bw)
                gp_dma(w3b.rearrange("p (c n) -> p c n", n=512),
                       moe_w3[ex].rearrange("(c p) n -> p c n", p=128),
                       sem=gwx)
                gp_dma(w2b.rearrange("p (c n) -> p c n", n=1024),
                       moe_w2[ex].rearrange("(c p) n -> p c n", p=128),
                       sem=gwx)
                wld = W((gwx, P.n(gwx)))
                for b in range(half * ACT_NBLK // 2,
                               (half + 1) * ACT_NBLK // 2):
                    xblk = xnT_blks[b % 2]
                    sbk = "SBK%d" % (b % 2)
                    for cc in range(NC8):
                        sy_load(xblk[:, cc * 512:(cc + 1) * 512],
                                sc_xnT[cc, :, b * 512:(b + 1) * 512],
                                sem=sbk,
                                waits=W(("TE", xblk_teM[b % 2]))
                                if cc == 0 else ())
                    xw = W((sbk, P.n(sbk)))
                    for ti in range(4):
                        tt = b * 4 + ti
                        ta = tt - ACT_NT // 2 * half
                        par = ti % 2
                        pg = psA if par == 0 else psE
                        pu = psB if par == 0 else psF
                        xts = [xblk[:, cc * 512 + ti * 128:
                                    cc * 512 + (ti + 1) * 128]
                               for cc in range(NC8)]
                        for cc in range(NC8):
                            tmm(pg[:], xts[cc], w1b[:, cc * 512:(cc + 1) * 512],
                                cc == 0, cc == 7, inc=False,
                                waits=(wld + xw +
                                       W(("SE", silm[par]),
                                         ("VE", hsttm[par])))
                                if cc == 0 else ())
                            tmm(pu[:], xts[cc], w3b[:, cc * 512:(cc + 1) * 512],
                                cc == 0, cc == 7, inc=(cc == 7))
                        hmm = teW()
                        sact(sil_sb[:], pg[:], AF.Silu,
                             waits=hmm + W(("VE", max(hsttm))))
                        silm[par] = P.n("SE")
                        if ex < 8:
                            vstt(h_sb[:], pu[:],
                                 gates_sb[:, tt * 8 + ex: tt * 8 + ex + 1],
                                 sil_sb[:], OP.mult, OP.mult,
                                 waits=W(("SE", silm[par]),
                                         ("TE", trTE[0])))
                        else:
                            vtt(h_sb[:], pu[:], sil_sb[:], OP.mult,
                                waits=W(("SE", silm[par]), ("TE", trTE[0])))
                        hsttm[par] = P.n("VE")
                        hdone = ("VE", hsttm[par])
                        for hc in range(4):
                            ttr(psT[:], h_sb[:, hc * 128:(hc + 1) * 128],
                                waits=W(hdone) + veW())
                            vcopy(hT_sb[:, hc * 128:(hc + 1) * 128], psT[:],
                                  waits=teW())
                        trTE[0] = P.n("TE")
                        psTm[0] = P.n("VE")
                        hTd = ("VE", psTm[0])
                        for hhalf, ps in ((0, psC), (1, psD)):
                            for hc in range(4):
                                tmm(ps[:], hT_sb[:, hc * 128:(hc + 1) * 128],
                                    w2b[:, hc * 1024 + hhalf * 512:
                                        hc * 1024 + (hhalf + 1) * 512],
                                    hc == 0, hc == 3, inc=(hc == 3),
                                    waits=W(hTd, ("VE", accm[0]))
                                    if hc == 0 else ())
                        omm = teW()
                        if ex == 0:
                            vcopy(accv[:, ta, 0:512], psC[:], inc=False,
                                  waits=omm)
                            vcopy(accv[:, ta, 512:1024], psD[:])
                        else:
                            vtt(accv[:, ta, 0:512], psC[:], accv[:, ta, 0:512],
                                OP.add, inc=False, waits=omm)
                            vtt(accv[:, ta, 512:1024], psD[:],
                                accv[:, ta, 512:1024], OP.add)
                        accm[0] = P.n("VE")
                    xblk_teM[b % 2] = P.n("TE")
                te_marks.append(P.n("TE"))

        def phase_D(half):
            accv = accbuf.rearrange("p (t n) -> p t n", n=C)
            slot_done = [0, 0]
            sy_barrier("SST")
            for ta in range(ACT_NT // 2):
                tt = ACT_NT // 2 * half + ta
                xs = x_sb[ta % 2]
                sx = "GX%d" % (ta % 2)
                gw = W(("VE", slot_done[ta % 2])) if ta >= 2 else ()
                gp_dma(xs[:], x_in[tt * 128:(tt + 1) * 128, :], sem=sx,
                       waits=gw)
                xw = W((sx, P.n(sx)))
                sy_load(s_sb[:], sc_s[tt, :, :], sem="SMS", waits=veW())
                sw = W(("SMS", P.n("SMS")))
                sact(g_sb[:], accv[:, ta, :], AF.Silu, waits=veW())
                vtt(x2_sb[:], s_sb[:], g_sb[:], OP.mult, inc=False,
                    waits=seW() + sw)
                vtt(x1_sb[:], xs[:], x2_sb[:], OP.add,
                    waits=xw + W(("SST", P.n("SST"))))
                slot_done[ta % 2] = P.n("VE")
                sy_store(sc_x1[tt, :, :], x1_sb[:], sem="SST", waits=veW())

        def phase_W():
            wstage = xnT_blks[1][:, 0:1024]
            psTcW = psC[:].bitcast(BF16)[:, 0:128]
            psTseW = [0, 0]
            specs = []
            specs.append(("g", gate_w, [(i, 0) for i in range(32)], sc_gq,
                          sin_r, 1024,
                          lambda i, j, ch: (ch, slice(i * 128, (i + 1) * 128))))
            specs.append(("u", up_w, [(i, 0) for i in range(32)], sc_uq,
                          sin_r, 1024,
                          lambda i, j, ch: (ch, slice(i * 128, (i + 1) * 128))))
            specs.append(("d", down_w, [(i, j) for i in range(8)
                                        for j in range(4)], sc_dq,
                          sind_r, 1024,
                          lambda i, j, ch: (j * 8 + ch, slice(i * 128, (i + 1) * 128))))
            sy_barrier("SS")
            for wname, wt, tiles, dst, sin_ap, ncols, dpos in specs:
                slot_done = [P.n("VE"), P.n("VE")]
                for n_, (i, j) in enumerate(tiles):
                    buf = wq_in[n_ % 2]
                    sx = "GX%d" % (n_ % 2)
                    gw = W(("VE", slot_done[n_ % 2]))
                    gp_dma(buf[:], wt[i * 128:(i + 1) * 128,
                                      j * 1024:(j + 1) * 1024], sem=sx,
                           waits=gw)
                    ldw = W((sx, P.n(sx)))
                    if n_ == 0:
                        vred(wmax_sb[:], buf[:], OP.max, absv=True, waits=ldw)
                    else:
                        vred(tmax_sb[:], buf[:], OP.max, absv=True,
                             inc=False, waits=ldw)
                        ve_spacer()
                        vtt(wmax_sb[:], wmax_sb[:], tmax_sb[:], OP.max)
                    slot_done[n_ % 2] = P.n("VE")
                sy_store(sc_p128, wmax_sb[:], waits=veW())
                sy_load(m128_sb[:], sc_p128, sem="SMS",
                        waits=W(("SS", P.n("SS"))))
                vred(m128_sb[:, 0:1], m128_sb[:], OP.max,
                     waits=W(("SMS", P.n("SMS"))))
                tmm(psS[:, 0:1], ones_f32[:], m128_sb[0:1, 0:1], True, True,
                    inc=True, waits=veW())
                vcopy(gmax_sb[:], psS[:, 0:1], inc=False, waits=teW())
                ve_spacer()
                vrec(wq_gs[wname][:], gmax_sb[:])
                vts(wq_gs6[wname][:], wq_gs[wname][:], 448.0, None, OP.mult,
                    inc=False)
                ve_spacer()
                vts(wq_gs[wname][:], wq_gs[wname][:], 2688.0, None, OP.mult,
                    inc=False)
                vrec(gs1[:], sin_ap[:])
                vts(gs2[:], gmax_sb[:], 1.0 / 2688.0, None, OP.mult, inc=False)
                ve_spacer()
                rdst = {"g": r_g, "u": r_u, "d": r_d}[wname]
                vtt(rdst[:], gs2[:], gs1[:], OP.mult)
                for n_, (i, j) in enumerate(tiles):
                    buf = wq_in[n_ % 2]
                    sx = "GX%d" % (n_ % 2)
                    gw = W(("VE", slot_done[n_ % 2]))
                    gp_dma(buf[:], wt[i * 128:(i + 1) * 128,
                                      j * 1024:(j + 1) * 1024], sem=sx,
                           waits=gw)
                    ldw = W((sx, P.n(sx)))
                    qd = emit_quant(buf[:], 1024, qout_sb,
                                    wq_gs[wname][:, 0:1],
                                    wq_gs6[wname][:, 0:1], waits_first=ldw)
                    slot_done[n_ % 2] = P.n("VE")
                    sguard = W(("SWG", P.n("SWG"))) if n_ >= 1 else []
                    for ch in range(8):
                        bk = psT[:] if ch % 2 == 0 else psTcW
                        ttr(bk, qout_sb[:, ch * 128:(ch + 1) * 128],
                            waits=W(qd, ("VE", psTseW[ch % 2])))
                        vcopy(wstage[:, ch * 128:(ch + 1) * 128], bk,
                              waits=teW() + (sguard if ch == 0 else []))
                        psTseW[ch % 2] = P.n("VE")
                    c0, csl = dpos(i, j, 0)
                    dview = dst[c0:c0 + 8, :, csl].rearrange("c p n -> p c n")
                    sy_store(dview,
                             wstage[:].rearrange("p (c n) -> p c n", n=128),
                             sem="SWG", waits=veW())

        def phase_F1():
            sy_barrier("SS", "SSG", "SWG")
            silm = [0]; hidm = [0]; qm = [0]; trTE = [0]; psTm = [0]
            psTc = psC[:].bitcast(BF16)[:, 0:128]
            psTse = [0, 0]
            xblk_teM = [0, 0]
            sil2 = x_sb[0][:]
            hid2 = x_sb[1][:]
            wq_teM = [0, 0]
            for ei in range(NC8):
                gq = wbufA if ei % 2 == 0 else wbufB
                uq = wbufC if ei % 2 == 0 else wbufD
                sq = "SQ%d" % (ei % 2)
                bw = W(("TE", wq_teM[ei % 2]))
                for cc in range(NC8):
                    sy_load(gq[:, cc * 512:(cc + 1) * 512],
                            sc_gq[cc, :, ei * 512:(ei + 1) * 512],
                            sem=sq, waits=bw if cc == 0 else ())
                    sy_load(uq[:, cc * 512:(cc + 1) * 512],
                            sc_uq[cc, :, ei * 512:(ei + 1) * 512], sem=sq)
                wld = W((sq, P.n(sq)))
                for b in range(ACT_NBLK):
                    xblk = xnT_blks[b % 2]
                    sbk = "SBK%d" % (b % 2)
                    for cc in range(NC8):
                        sy_load(xblk[:, cc * 512:(cc + 1) * 512],
                                sc_xqT[cc, :, b * 512:(b + 1) * 512],
                                sem=sbk,
                                waits=W(("TE", xblk_teM[b % 2]))
                                if cc == 0 else ())
                    xw = W((sbk, P.n(sbk)))
                    for pj in range(2):
                        for half in range(2):
                            ti = pj * 2 + half
                            pg = psA if half == 0 else psE
                            pu = psB if half == 0 else psF
                            xts = [xblk[:, cc * 512 + ti * 128:
                                        cc * 512 + (ti + 1) * 128]
                                   for cc in range(NC8)]
                            for cc in range(NC8):
                                tmm(pg[:], xts[cc],
                                    gq[:, cc * 512:(cc + 1) * 512],
                                    cc == 0, cc == 7, inc=False,
                                    waits=(wld + xw +
                                           W(("SE", silm[0]),
                                             ("VE", hidm[0])))
                                    if cc == 0 else ())
                                tmm(pu[:], xts[cc],
                                    uq[:, cc * 512:(cc + 1) * 512],
                                    cc == 0, cc == 7, inc=(cc == 7))
                        hmm = teW()
                        sact(sil2[:, 0:512], psA[:], AF.Silu,
                             scale=r_g[:, 0:1], waits=hmm + W(("VE", qm[0])))
                        sact(sil2[:, 512:1024], psE[:], AF.Silu,
                             scale=r_g[:, 0:1])
                        silm[0] = P.n("SE")
                        vts(hid2[:, 0:512], psB[:], r_u[:, 0:1], None,
                            OP.mult, inc=False,
                            waits=W(("SE", silm[0]), ("VE", qm[0])))
                        vts(hid2[:, 512:1024], psF[:], r_u[:, 0:1], None,
                            OP.mult, inc=False)
                        vtt(hid2[:, 0:512], hid2[:, 0:512], sil2[:, 0:512],
                            OP.mult, inc=False)
                        vtt(hid2[:, 512:1024], hid2[:, 512:1024],
                            sil2[:, 512:1024], OP.mult)
                        hidm[0] = P.n("VE")
                        qd = emit_quant(hid2[:], 1024, hq_stage[:, 0:1024],
                                        sind_r[:, 0:1], sind6[:, 0:1])
                        qm[0] = P.n("VE")
                        sguard = W(("SSG", P.n("SSG")))
                        for qc in range(8):
                            bk = psT[:] if qc % 2 == 0 else psTc
                            ttr(bk, hq_stage[:, qc * 128:(qc + 1) * 128],
                                waits=W(qd, ("VE", psTse[qc % 2])))
                            vcopy(hq_stage[:, 1024 + qc * 128:
                                           1024 + (qc + 1) * 128], bk,
                                  waits=teW() + (sguard if qc == 0 else []))
                            psTse[qc % 2] = P.n("VE")
                        psTm[0] = P.n("VE")
                        for half in range(2):
                            tt = b * 4 + pj * 2 + half
                            sy_store(sc_hqT[tt, ei * 4:(ei + 1) * 4, :, :]
                                     .rearrange("c p n -> p c n"),
                                     hq_stage[:, 1024 + half * 512:
                                              1024 + (half + 1) * 512]
                                     .rearrange("p (c n) -> p c n", n=128),
                                     sem="SSG", waits=W(("VE", psTm[0])))
                    xblk_teM[b % 2] = P.n("TE")
                wq_teM[ei % 2] = P.n("TE")

        def phase_F2():
            dq = downq
            sy_barrier("SSG", "SST")
            mark = P.n("TE")
            for ic in range(32):
                sy_load(dq[:, ic * 1024:(ic + 1) * 1024], sc_dq[ic, :, :],
                        sem="SMS", waits=W(("TE", mark)) if ic == 0 else ())
            dld = W(("SMS", P.n("SMS")))
            slot_done = [0, 0]
            for tt in range(ACT_NT):
                hb = xnT_blks[tt % 2]
                sbk = "SBK%d" % (tt % 2)
                gw = W(("TE", slot_done[tt % 2])) if tt >= 2 else ()
                sy_load(hb[:].rearrange("p (c n) -> p c n", n=128),
                        sc_hqT[tt, :, :, :].rearrange("c p n -> p c n"),
                        sem=sbk, waits=gw)
                sy_load(s_sb[:], sc_s[tt, :, :], sem="SMS", waits=veW())
                sy_load(x1_sb[:], sc_x1[tt, :, :], sem="SMS")
                hw = W((sbk, P.n(sbk)), ("SMS", P.n("SMS")))
                for half, ps in ((0, psC), (1, psD)):
                    for ic in range(32):
                        tmm(ps[:], hb[:, ic * 128:(ic + 1) * 128],
                            dq[:, ic * 1024 + half * 512:
                               ic * 1024 + (half + 1) * 512],
                            ic == 0, ic == 31, inc=(ic == 31),
                            waits=(dld + hw + veW() + seW()) if ic == 0 else ())
                omm = teW()
                slot_done[tt % 2] = P.n("TE")
                sact(g_sb[:, 0:512], psC[:], AF.Silu, scale=r_d[:, 0:1],
                     waits=omm)
                sact(g_sb[:, 512:1024], psD[:], AF.Silu, scale=r_d[:, 0:1])
                vtt(x2_sb[:], s_sb[:], g_sb[:], OP.mult, inc=False,
                    waits=seW() + hw)
                vtt(x2_sb[:], x1_sb[:], x2_sb[:], OP.add)
                sy_store(out[tt * 128:(tt + 1) * 128, :], x2_sb[:],
                         waits=veW())

        for cc in range(NC8):
            gp_dma(wres[:, cc * 3072:(cc + 1) * 3072],
                   l0_wqkvT[cc * 128:(cc + 1) * 128, :])
        for cc in range(NC8):
            gp_dma(wres2[:, cc * 1024:(cc + 1) * 1024],
                   l0_wprojT[cc * 128:(cc + 1) * 128, :])
        gp_dma(bias_row[:], l0_bproj)
        gp_dma(rtb_row[:], rt_b)
        for cc in range(NC8):
            gp_dma(rtw_sb[:, cc * 8:(cc + 1) * 8],
                   rt_w[cc * 128:(cc + 1) * 128, :])
        l0w = P.n("GWA")
        P.emit("te", lambda e, v=l0w: e.wait_ge(P.S["GWA"], v), inc=None)
        vts(sin6[:], sin_r[:], 1.0 / 6.0, None, OP.mult,
            waits=W(("GWA", l0w)))
        vts(sind6[:], sind_r[:], 1.0 / 6.0, None, OP.mult)

        phase_A(0)
        if _sge("l0b"):
            phase_B(0)
        if _sge("l0c"):
            phase_C_moe(0)
        if _sge("l0"):
            phase_D(0)
            phase_C_moe(1)
            phase_D(1)
        if _sge("w"):
            phase_W()
        if _sge("l1a"):
            mark = P.n("TE")
            P.emit("gp", lambda e: e.wait_ge(P.S["TE"], mark), inc=None)
            for cc in range(NC8):
                gp_dma(wres[:, cc * 3072:cc * 3072 + 1024],
                       l1_wq[cc * 128:(cc + 1) * 128, :])
                gp_dma(wres[:, cc * 3072 + 1024:cc * 3072 + 2048],
                       l1_wk[cc * 128:(cc + 1) * 128, :])
                gp_dma(wres[:, cc * 3072 + 2048:cc * 3072 + 3072],
                       l1_wv[cc * 128:(cc + 1) * 128, :])
            for cc in range(NC8):
                gp_dma(wres2[:, cc * 1024:(cc + 1) * 1024],
                       l1_wo[cc * 128:(cc + 1) * 128, :])
            gp_dma(bias_row[:], l1_bo)
            l1w = P.n("GWA")
            P.emit("te", lambda e, v=l1w: e.wait_ge(P.S["GWA"], v), inc=None)
            sy_barrier("SST")
            phase_A(1)
        if _sge("l1b"):
            phase_B(1)
        if _sge("full"):
            phase_F1()
            phase_F2()

        if BUILD_STAGE != "full":
            ve(lambda e: e.memset(x2_sb[:], 0.0))
            for tt in range(NT):
                sy_store(out[tt * 128:(tt + 1) * 128, :], x2_sb[:],
                         waits=veW())
        for sm_ in ("SS", "SSG", "SWG", "SST", "SXN0", "SXN1", "SQT0",
                    "SQT1"):
            P.emit("sy", lambda e, sm=sm_, v=P.cnt[sm_]: e.wait_ge(P.S[sm], v)
                   if v > 0 else e.nop(), inc=None)
        P.emit("gp", lambda e, v=P.cnt["SS"]: e.wait_ge(P.S["SS"], v),
               inc=None)

        block = st.enter_context(nc.Block())

        @block.gpsimd
        def _(e):
            for f in P.q["gp"]:
                f(e)

        @block.sync
        def _(e):
            for f in P.q["sy"]:
                f(e)

        @block.tensor
        def _(e):
            for f in P.q["te"]:
                f(e)

        @block.vector
        def _(e):
            for f in P.q["ve"]:
                f(e)

        @block.scalar
        def _(e):
            for f in P.q["se"]:
                f(e)

    return nc


def _prep_inputs(inp):
    f32 = np.float32
    ct = lambda a: np.ascontiguousarray(np.asarray(a), dtype=f32)
    x = ct(inp["L0_attn_x"])
    base = {
        "l0_wqkvT": ct(np.asarray(inp["L0_attn_Wqkv"]).T),
        "l0_wprojT": ct(np.asarray(inp["L0_attn_Wproj"]).T),
        "l0_bproj": ct(inp["L0_attn_bproj"]).reshape(1, C),
        "rt_w": ct(inp["L0_ffn_router_w"]),
        "rt_b": ct(inp["L0_ffn_router_b"]).reshape(1, 8),
        "moe_w1": np.concatenate([ct(inp["L0_ffn_routed_w1"]),
                                  ct(inp["L0_ffn_shared_w1"])], axis=0),
        "moe_w3": np.concatenate([ct(inp["L0_ffn_routed_w3"]),
                                  ct(inp["L0_ffn_shared_w3"])], axis=0),
        "moe_w2": np.concatenate([ct(inp["L0_ffn_routed_w2"]),
                                  ct(inp["L0_ffn_shared_w2"])], axis=0),
        "l1_wq": ct(inp["L1_attn_Wq"]),
        "l1_wk": ct(inp["L1_attn_Wk"]),
        "l1_wv": ct(inp["L1_attn_Wv"]),
        "l1_wo": ct(inp["L1_attn_Wo"]),
        "l1_bo": ct(inp["L1_attn_bo"]).reshape(1, C),
        "l1_wf2": np.concatenate([ct(inp["L1_attn_Wf"])] * 2, axis=0),
        "l1_bf2": np.tile(ct(inp["L1_attn_bf"]), 2).reshape(128, 1),
        "gate_w": ct(inp["L1_ffn_gate_w"]),
        "up_w": ct(inp["L1_ffn_up_w"]),
        "down_w": ct(inp["L1_ffn_down_w"]),
        "s_in_rep": np.full((128, 1), ct(inp["L1_ffn_s_in"]).reshape(-1)[0], f32),
        "s_ind_rep": np.full((128, 1), ct(inp["L1_ffn_s_in_down"]).reshape(-1)[0], f32),
        "ident": np.eye(128, dtype=f32),
    }
    return [dict(base, x=ct(x[i])) for i in range(8)]


_LAST_RESULTS = [None]


def kernel(**inputs):
    maps = _prep_inputs(inputs)
    nc = build_program()
    res = run_bass_kernel_spmd(nc, maps, core_ids=list(range(8)))
    _LAST_RESULTS[0] = res
    return np.stack([np.asarray(r["out"], dtype=np.float32)
                     for r in res.results], axis=0)
